# version 04 (final)
# baseline (speedup 1.0000x reference)
import contextlib
import numpy as np
import ml_dtypes
import concourse.bass as bass
import concourse.mybir as mybir
from concourse.bass_utils import run_bass_kernel_spmd

F32 = mybir.dt.float32
BF16 = mybir.dt.bfloat16
AF = mybir.ActivationFunctionType
ALU = mybir.AluOpType
BF = ml_dtypes.bfloat16

P = 128
T = 2048
D = 1024
FCH = 8
DI = 2048
ICH = 16
HID = 4096
MCH = 32
NH = 32
HD = 64
DS = 64
CHK = 128
NCHK = 16
GT = 512
NG = 4
CPG = 4
TB = 1024
LTN = 512
EPS = 1e-5

N_SUB = 3
import os as _os
KCS = int(_os.environ.get("KCS", "7"))


DIRTY = set()


class Stream:
    def __init__(self, nc, name):
        self.h = nc.alloc_semaphore(name=name)
        self.n = 0

    def inc(self, instr, k=1):
        instr.then_inc(self.h, k)
        self.n += k
        DIRTY.add(self)
        return instr


def rep_free(ap, reps):
    return ap.unsqueeze(len(ap.shape)).broadcast_to(list(ap.shape) + [reps])


def build_nc(n_sub=N_SUB, m_stage=6):
    nc = bass.Bass("TRN2", target_bir_lowering=False, debug=False)

    def din(name, shape, dt=F32):
        return nc.dram_tensor(name, list(shape), dt, kind="ExternalInput").ap()

    xin = din("xin", [P, FCH, T])
    WinT = din("WinT", [D, 4256])
    WoutT = din("WoutT", [DI, D])
    w1e = din("w1e", [2, D, HID])
    w2e = din("w2e", [2, HID, D])
    sw1 = din("sw1", [D, HID])
    sw2 = din("sw2", [HID, D])
    gwT = din("gwT", [P, FCH, 2])
    b1p = din("b1p", [2, P, MCH])
    sb1p = din("sb1p", [P, MCH])
    b2p = din("b2p", [2, P, FCH])
    sb2p = din("sb2p", [P, FCH])
    convw = din("convw", [P, 18, 4])
    convb = din("convb", [P, 18])
    dtb = din("dtb", [32, 1])
    alog = din("alog", [32, 1])
    Dbc = din("Dbc", [P, DI], BF16)
    nwp = din("nwp", [P, ICH])
    ones_bf = din("ones_bf", [1, P], BF16)
    onesc_bf = din("onesc_bf", [P, 1], BF16)
    ident_bf = din("ident_bf", [P, P], BF16)
    ident_f32 = din("ident_f32", [P, P])
    ustrict = din("ustrict", [P, P], BF16)
    uincl = din("uincl", [P, P], BF16)
    sel = din("sel", [P, 8])
    ehc2 = din("ehc2", [32, ICH, P], BF16)

    yout = nc.dram_tensor("yout", [P, FCH, T], F32, kind="ExternalOutput").ap()

    ccA_in = nc.dram_tensor("ccA_in", [P, 64], F32, kind="Internal").ap()
    ccA_out = nc.dram_tensor("ccA_out", [8, P, 64], F32, kind="Internal").ap()
    ccB_in = nc.dram_tensor("ccB_in", [64, NH * HD], BF16, kind="Internal").ap()
    ccB_out = nc.dram_tensor("ccB_out", [8, 64, NH * HD], BF16, kind="Internal").ap()
    zspill = nc.dram_tensor("zspill", [DI, T], BF16, kind="Internal").ap()
    yspill = nc.dram_tensor("yspill", [P, NCHK, DI], BF16, kind="Internal").ap()

    St = {}
    for name in ["IN", "BV", "PH", "SQ", "SQC", "LNR", "LNB", "LNX",
                 "GS", "MSK", "XE", "W1A", "W1B", "W2", "HP", "H", "OPW", "OF", "XR",
                 "WI", "ZEV", "DTE", "CV", "CVS", "CAR", "ROW", "TRP", "TRS",
                 "GTP", "GM", "LH", "SGP", "EXP", "M4", "XDP", "SDP", "CH4",
                 "YDQ", "YOQ", "STQ", "SUP", "YEV", "ERP",
                 "HSRC", "CCA", "HGA", "HLO", "SFIN", "CCB", "SIN",
                 "LAM", "SCS", "CORP", "COREV", "ZR", "ZRB", "ZRC", "ZRD", "PPQ", "RMS", "YSP", "SINB", "SGA",
                 "YFP", "YFS", "WOD", "WOP", "ZSA", "ZSB", "OUT"]:
        St[name] = Stream(nc, "s" + name)

    ctx = contextlib.ExitStack()
    sbt = lambda name, shape, dt=F32: ctx.enter_context(nc.sbuf_tensor(name, list(shape), dt))

    xbuf = sbt("xbuf", [P, FCH, T])
    onesb = sbt("onesb", [1, P], BF16)
    onesc = sbt("onesc", [P, 1], BF16)
    identb = sbt("identb", [P, P], BF16)
    identf = sbt("identf", [P, P])
    ustr = sbt("ustr", [P, P], BF16)
    uinc = sbt("uinc", [P, P], BF16)
    convw_s = sbt("convw_s", [P, 18, 4])
    convb_s = sbt("convb_s", [P, 18])
    dtb_s = sbt("dtb_s", [32, 1])
    alog_s = sbt("alog_s", [32, 1])
    nA_s = sbt("nA_s", [32, 1])
    nwp_s = sbt("nwp_s", [P, ICH])
    gwT_s = sbt("gwT_s", [P, FCH, 2])
    gwd_s = sbt("gwd_s", [P, FCH], BF16)
    b1h_s = sbt("b1h_s", [P, 3, MCH])
    bsum_s = sbt("bsum_s", [P, FCH])
    sel_s = sbt("sel_s", [P, 8])
    ehc_s = sbt("ehc_s", [32, ICH, P], BF16)
    ndtb_s = sbt("ndtb_s", [32, 1])
    rows = sbt("rows", [1, 1536])
    stg2 = sbt("stg2", [P, 2, FCH])
    rowsbf = sbt("rowsbf", [1, 1024], BF16)
    tmpc = sbt("tmpc", [P, 2, LTN], BF16)
    sqb = sbt("sqb", [P, 2, LTN], BF16)

    pa = [ctx.enter_context(nc.psum_tensor(f"pa{i}", [P, 512], F32)) for i in range(6)]
    pt = [ctx.enter_context(nc.psum_tensor(f"pt{i}", [P, 1024], BF16)) for i in range(2)]

    blk = ctx.enter_context(nc.Block())

    def barrier():
        base = St["PH"].n
        waits = [(st.h, st.n) for st in DIRTY]
        DIRTY.clear()
        for e in ["tensor", "vector", "scalar", "gpsimd", "sync"]:
            def body(eng, _b=base, _w=waits):
                for h, n in _w:
                    eng.wait_ge(h, n)
                eng.sem_inc(St["PH"].h, 1)
                eng.wait_ge(St["PH"].h, _b + 5)
            getattr(blk, e)(body)
        St["PH"].n = base + 5

    def phase_load():
        def sy_body(sy):
            St["IN"].inc(sy.dma_start(xbuf[:, :, :], xin), 16)
            for dst, src in [(onesb[:, :], ones_bf), (onesc[:, :], onesc_bf), (identb[:, :], ident_bf),
                             (identf[:, :], ident_f32), (ustr[:, :], ustrict),
                             (uinc[:, :], uincl), (convw_s[:, :, :], convw),
                             (convb_s[:, :], convb), (dtb_s[:, :], dtb),
                             (alog_s[:, :], alog),
                             (nwp_s[:, :], nwp), (gwT_s[:, :, :], gwT),
                             (sel_s[:, :], sel), (ehc_s[:, :, :], ehc2),
                             (b1h_s[:, 0, :], b1p[0]), (b1h_s[:, 1, :], b1p[1]),
                             (b1h_s[:, 2, :], sb1p), (bsum_s[:, :], b2p[0]),
                             (stg2[:, 0, :], b2p[1]), (stg2[:, 1, :], sb2p)]:
                St["IN"].inc(sy.dma_start(dst, src), 16)
        blk.sync(sy_body)
        n_in = St["IN"].n

        def s_body(s):
            s.wait_ge(St["IN"].h, n_in)
            St["BV"].inc(s.activation(nA_s[:, :], alog_s[:, :], AF.Exp))
            St["BV"].inc(s.activation(ndtb_s[:, :], dtb_s[:, :], AF.Copy, scale=-1.0))
        blk.scalar(s_body)

        def v_body(v):
            v.wait_ge(St["IN"].h, n_in)
            v.tensor_tensor(gwd_s[:, :], gwT_s[:, :, 0], gwT_s[:, :, 1], ALU.subtract)
            St["BV"].inc(v.tensor_add(bsum_s[:, :], bsum_s[:, :], stg2[:, 0, :]))
            v.wait_ge(St["BV"].h, St["BV"].n)
            St["BV"].inc(v.tensor_add(bsum_s[:, :], bsum_s[:, :], stg2[:, 1, :]))
            v.wait_ge(St["BV"].h, St["BV"].n)
            St["BV"].inc(v.tensor_scalar(bsum_s[:, :], bsum_s[:, :], 0.5, None, ALU.mult))
            St["BV"].inc(v.tensor_scalar(b1h_s[:, :, :], b1h_s[:, :, :], 0.5, None, ALU.mult))
        blk.vector(v_body)
        barrier()

    def ln_tile(t0, xn_ap, w=LTN):
        sq_base, sqc_base = St["SQ"].n, St["SQC"].n
        lnx_guard = St["LNX"].n

        def s_body(s):
            for f in range(FCH):
                if f >= 1:
                    s.wait_ge(St["SQC"].h, sqc_base + f)
                St["SQ"].inc(s.activation(sqb[:, 0, 0:w], xbuf[:, f, t0:t0 + w], AF.Copy))
                St["SQ"].inc(s.activation(sqb[:, 1, 0:w], xbuf[:, f, t0:t0 + w], AF.Square))
        blk.scalar(s_body)

        def t_body(t):
            t.wait_ge(St["LNX"].h, lnx_guard)
            for f in range(FCH):
                t.wait_ge(St["SQ"].h, sq_base + 2 * (f + 1))
                t.matmul(pa[4][0:1, 0:w], onesc[:, :], sqb[:, 0, 0:w],
                         start=(f == 0), stop=(f == FCH - 1))
                St["SQC"].inc(t.matmul(pa[5][0:1, 0:w], onesc[:, :], sqb[:, 1, 0:w],
                                       start=(f == 0), stop=(f == FCH - 1)))
        blk.tensor(t_body)
        n_sqc = St["SQC"].n

        def v_body(v):
            v.wait_ge(St["SQC"].h, n_sqc)
            v.tensor_scalar(rows[:, 0:w], pa[4][0:1, 0:w], 1.0 / D, None, ALU.mult)
            St["ROW"].inc(v.tensor_scalar(rows[:, LTN:LTN + w], pa[5][0:1, 0:w], 1.0 / D, None, ALU.mult))
            v.wait_ge(St["ROW"].h, St["ROW"].n)
            St["ROW"].inc(v.tensor_tensor(rows[:, 2 * LTN:2 * LTN + w], rows[:, 0:w], rows[:, 0:w], ALU.mult))
            v.wait_ge(St["ROW"].h, St["ROW"].n)
            St["ROW"].inc(v.tensor_tensor(rows[:, LTN:LTN + w], rows[:, LTN:LTN + w],
                                          rows[:, 2 * LTN:2 * LTN + w], ALU.subtract))
            v.wait_ge(St["ROW"].h, St["ROW"].n)
            St["ROW"].inc(v.tensor_scalar(rows[:, LTN:LTN + w], rows[:, LTN:LTN + w],
                                          EPS, None, ALU.add))
            v.wait_ge(St["ROW"].h, St["ROW"].n)
            St["ROW"].inc(v.reciprocal(rows[:, 2 * LTN:2 * LTN + w], rows[:, LTN:LTN + w]))
        blk.vector(v_body)
        n_row = St["ROW"].n

        def s2_body(s):
            s.wait_ge(St["ROW"].h, n_row)
            St["ROW"].inc(s.activation(rows[:, LTN:LTN + w], rows[:, 2 * LTN:2 * LTN + w], AF.Sqrt))
            s.wait_ge(St["ROW"].h, St["ROW"].n)
            St["ROW"].inc(s.activation(rowsbf[:, 0:w], rows[:, 0:w], AF.Copy))
            St["ROW"].inc(s.activation(rowsbf[:, LTN:LTN + w], rows[:, LTN:LTN + w], AF.Copy))
        blk.scalar(s2_body)
        n_row2 = St["ROW"].n

        def t2_body(t):
            t.wait_ge(St["ROW"].h, n_row2)
            t.matmul(pa[4][:, 0:w], onesb[:, :], rowsbf[:, 0:w], start=True, stop=True)
            St["LNB"].inc(t.matmul(pa[5][:, 0:w], onesb[:, :], rowsbf[:, LTN:LTN + w], start=True, stop=True))
        blk.tensor(t2_body)
        n_lnb = St["LNB"].n

        def v2_body(v):
            v.wait_ge(St["LNB"].h, n_lnb)
            lna0 = St["LNX"].n
            for f in range(FCH):
                if f >= 2:
                    v.wait_ge(St["LNX"].h, lna0 + 2 * (f - 1))
                St["LNX"].inc(v.tensor_tensor(tmpc[:, f % 2, 0:w], xbuf[:, f, t0:t0 + w], pa[4][:, 0:w], ALU.subtract))
                v.wait_ge(St["LNX"].h, St["LNX"].n)
                St["LNX"].inc(v.tensor_tensor(xn_ap(f), tmpc[:, f % 2, 0:w], pa[5][:, 0:w], ALU.mult))
        blk.vector(v2_body)

    def ffn_sublayer(tag):
        with (nc.sbuf_tensor(f"xn{tag}", [P, FCH, TB], BF16) as xn,
              nc.sbuf_tensor(f"xe{tag}", [P, FCH, TB], BF16) as xe,
              nc.sbuf_tensor(f"hb{tag}", [P, 16, TB], BF16) as hbuf,
              nc.sbuf_tensor(f"w1b{tag}", [P, 2, FCH, P], BF16) as w1b,
              nc.sbuf_tensor(f"w2g{tag}", [P, 16, D], BF16) as w2g,
              nc.sbuf_tensor(f"mr{tag}", [1, TB], BF16) as mrow):
            for tb in range(2):
                t0 = tb * TB
                ln_tile(t0, lambda f: xn[:, f, 0:LTN])
                ln_tile(t0 + LTN, lambda f: xn[:, f, LTN:TB])
                n_lnx = St["LNX"].n

                for hh in range(2):
                    sl = slice(hh * LTN, (hh + 1) * LTN)
                    n_g0 = St["GS"].n
                    def t_g(t, _sl=sl, _hh=hh, _n=n_g0):
                        t.wait_ge(St["LNX"].h, n_lnx)
                        if _hh == 1:
                            t.wait_ge(St["GS"].h, _n)
                        for f in range(FCH):
                            i = t.matmul(pa[4][0:1, 0:LTN], gwd_s[:, f:f + 1],
                                         xn[:, f, _sl], start=(f == 0), stop=(f == FCH - 1))
                        St["GS"].inc(i)
                    blk.tensor(t_g)
                    n_g = St["GS"].n

                    def v_g(v, _hh=hh, _n=n_g):
                        v.wait_ge(St["GS"].h, _n)
                        St["GS"].inc(v.tensor_scalar(mrow[:, _hh * LTN:(_hh + 1) * LTN],
                                                     pa[4][0:1, 0:LTN], 0.0, None, ALU.is_ge))
                    blk.vector(v_g)

                n_gs = St["GS"].n
                def t_mb(t):
                    t.wait_ge(St["GS"].h, n_gs)
                    t.wait_ge(St["LNX"].h, n_lnx)
                    t.matmul(pa[4][:, 0:LTN], onesb[:, :], mrow[:, 0:LTN], start=True, stop=True)
                    St["MSK"].inc(t.matmul(pa[5][:, 0:LTN], onesb[:, :], mrow[:, LTN:TB], start=True, stop=True))
                blk.tensor(t_mb)
                n_msk = St["MSK"].n

                def v_xe(v):
                    v.wait_ge(St["MSK"].h, n_msk)
                    for f in range(FCH):
                        for hh in range(2):
                            St["XE"].inc(v.tensor_tensor(
                                xe[:, f, hh * LTN:(hh + 1) * LTN],
                                xn[:, f, hh * LTN:(hh + 1) * LTN],
                                pa[4 + hh][:, 0:LTN], ALU.mult))
                blk.vector(v_xe)

                for path in range(3):
                    if path == 1:
                        n_hp_prev = St["HP"].n
                        def v_xe1(v):
                            v.wait_ge(St["HP"].h, n_hp_prev)
                            for f in range(FCH):
                                St["XE"].inc(v.tensor_tensor(
                                    xe[:, f, :], xn[:, f, :], xe[:, f, :], ALU.subtract))
                        blk.vector(v_xe1)
                    rhs = xe if path < 2 else xn
                    n_xe = St["XE"].n
                    wsrc1 = w1e[path] if path < 2 else sw1
                    wsrc2 = w2e[path] if path < 2 else sw2
                    bslot = path if path < 2 else 2

                    for grp in range(2):
                        m0 = grp * 16
                        w1_base = [St["W1A"].n, St["W1B"].n]
                        h_base = St["H"].n
                        def g_w1(g, _w=wsrc1, _m0=m0, _hb=h_base):
                            for mi in range(16):
                                m = _m0 + mi
                                if _hb + 2 * (mi - 1) > 0:
                                    g.wait_ge(St["H"].h, _hb + 2 * (mi - 1))
                                St["W1A" if m % 2 == 0 else "W1B"].inc(g.dma_start(
                                    w1b[:, m % 2],
                                    _w[:, m * P:(m + 1) * P].rearrange("(kc p) j -> p kc j", p=P)), 16)
                        blk.gpsimd(g_w1)

                        w2_base = St["W2"].n
                        opw_guard = St["OPW"].n
                        def g_w2(g, _w=wsrc2, _m0=m0, _gd=opw_guard):
                            g.wait_ge(St["OPW"].h, _gd)
                            for mi in range(16):
                                m = _m0 + mi
                                St["W2"].inc(g.dma_start(w2g[:, mi], _w[m * P:(m + 1) * P, :]), 16)
                        blk.gpsimd(g_w2)

                        hp_base = St["HP"].n
                        of_guard = St["OF"].n
                        def t_w1(t, _m0=m0, _hb=h_base, _w1b=w1_base, _of=of_guard):
                            t.wait_ge(St["XE"].h, n_xe)
                            t.wait_ge(St["OF"].h, _of)
                            for mi in range(16):
                                m = _m0 + mi
                                par = "W1A" if m % 2 == 0 else "W1B"
                                t.wait_ge(St[par].h, _w1b[m % 2] + 16 * (mi // 2 + 1))
                                if mi >= 2:
                                    t.wait_ge(St["H"].h, _hb + 2 * (mi - 1))
                                for hh in range(2):
                                    bank = pa[(mi % 2) * 2 + hh]
                                    for f in range(FCH):
                                        i = t.matmul(bank[:, 0:512],
                                                     w1b[:, (_m0 + mi) % 2, f, :],
                                                     rhs[:, f, hh * LTN:(hh + 1) * LTN],
                                                     start=(f == 0), stop=(f == FCH - 1))
                                    St["HP"].inc(i)
                        blk.tensor(t_w1)

                        def s_h(s, _m0=m0, _hp=hp_base, _bs=bslot):
                            for mi in range(16):
                                for hh in range(2):
                                    s.wait_ge(St["HP"].h, _hp + 2 * mi + hh + 1)
                                    bank = pa[(mi % 2) * 2 + hh]
                                    St["H"].inc(s.activation(
                                        hbuf[:, mi, hh * LTN:(hh + 1) * LTN], bank[:, 0:512],
                                        AF.Relu, bias=b1h_s[:, _bs, (_m0 + mi):(_m0 + mi) + 1],
                                        scale=0.5))
                        blk.scalar(s_h)
                        n_hdone = St["H"].n

                        op_base = St["OPW"].n
                        of_base = St["OF"].n
                        def t_w2(t, _w2b=w2_base, _nh=n_hdone, _of=of_base):
                            t.wait_ge(St["H"].h, _nh)
                            t.wait_ge(St["W2"].h, _w2b + 16 * 16)
                            for d in range(FCH):
                                if d >= 2:
                                    t.wait_ge(St["OF"].h, _of + 2 * (d - 1))
                                for hh in range(2):
                                    bank = pa[(d % 2) * 2 + hh]
                                    for mi in range(16):
                                        i = t.matmul(bank[:, 0:512],
                                                     w2g[:, mi, d * P:(d + 1) * P],
                                                     hbuf[:, mi, hh * LTN:(hh + 1) * LTN],
                                                     start=(mi == 0), stop=(mi == 15))
                                    St["OPW"].inc(i)
                        blk.tensor(t_w2)

                        def v_fl(v, _t0=t0, _op=op_base):
                            for d in range(FCH):
                                for hh in range(2):
                                    v.wait_ge(St["OPW"].h, _op + 2 * d + hh + 1)
                                    bank = pa[(d % 2) * 2 + hh]
                                    sl = slice(_t0 + hh * LTN, _t0 + (hh + 1) * LTN)
                                    St["OF"].inc(v.tensor_tensor(
                                        xbuf[:, d, sl], xbuf[:, d, sl], bank[:, 0:512], ALU.add))
                        blk.vector(v_fl)
                n_of = St["OF"].n
                def v_bias(v, _t0=t0):
                    v.wait_ge(St["OF"].h, n_of)
                    for f in range(FCH):
                        St["XR"].inc(v.tensor_scalar(
                            xbuf[:, f, _t0:_t0 + TB], xbuf[:, f, _t0:_t0 + TB],
                            bsum_s[:, f:f + 1], None, ALU.add))
                blk.vector(v_bias)
            barrier()

    def emit_chunk(c, cl, csl, g, t0, xcv, bccv, xtm, ttm, dcyf, lh4, e4b, m4b,
                   gmb, ch2b, xdtb, xddb, xDq, stmp, erep,
                   u_r, adt_r, acs_r, sdob, ytm, Sst, Sbf, Call2, Btm,
                   yspill, Dbc_s, n_dbc, n_cvs, n_dt3, n_erp, first_chunk):
        trp_b, trs_b = St["TRP"].n, St["TRS"].n
        lh_b, sgp_b, exp_b, m4_b = St["LH"].n, St["SGP"].n, St["EXP"].n, St["M4"].n
        ydq_b, sdp_b, ch4_b, yoq_b = St["YDQ"].n, St["SDP"].n, St["CH4"].n, St["YOQ"].n
        stq_b, sup_b, yev_b, ysp_b = St["STQ"].n, St["SUP"].n, St["YEV"].n, St["YSP"].n
        xdp_b, gm_b = St["XDP"].n, St["GM"].n

        def s_dcy(s):
            s.wait_ge(St["DTE"].h, n_dt3)
            St["DTE"].inc(s.activation(dcyf[:, :], acs_r[:, csl], AF.Exp,
                                       scale=-1.0, bias=acs_r[:, csl.stop - 1:csl.stop]))
        blk.scalar(s_dcy)
        n_dcy = St["DTE"].n

        def t_tr(t):
            t.wait_ge(St["CVS"].h, n_cvs)
            t.wait_ge(St["DTE"].h, n_dcy)
            if trs_b >= 19:
                t.wait_ge(St["TRS"].h, trs_b)
            if xdp_b >= 4:
                t.wait_ge(St["XDP"].h, xdp_b)
            if m4_b >= 8:
                t.wait_ge(St["M4"].h, m4_b)
            for f in range(8):
                St["TRP"].inc(t.transpose(pt[0][:, f * P:(f + 1) * P],
                                          xcv[:, f, cl * CHK:(cl + 1) * CHK], identb[:, :]))
            for f in range(8):
                St["TRP"].inc(t.transpose(pt[1][:, f * P:(f + 1) * P],
                                          xcv[:, 8 + f, cl * CHK:(cl + 1) * CHK], identb[:, :]))
            t.wait_ge(St["TRS"].h, trs_b + 8)
            St["TRP"].inc(t.transpose(pt[0][:, 0:DS], bccv[0:64, 0, cl * CHK:(cl + 1) * CHK],
                                      identb[0:64, 0:64]))
            St["TRP"].inc(t.transpose(pt[0][:, DS:DS + NH], dcyf[:, :], identb[0:32, 0:32]))
            St["TRP"].inc(t.transpose(pa[5][:, 0:NH], adt_r[:, csl], identf[0:32, 0:32]))
            St["TRP"].inc(t.transpose(pa[5][:, NH:2 * NH], u_r[:, csl], identf[0:32, 0:32]))
        blk.tensor(t_tr)

        def s_trev(s):
            s.wait_ge(St["TRP"].h, trp_b + 8)
            for f in range(8):
                St["TRS"].inc(s.activation(xtm[:, cl, f * P:(f + 1) * P],
                                           pt[0][:, f * P:(f + 1) * P], AF.Copy))
            s.wait_ge(St["TRP"].h, trp_b + 16)
            for f in range(8):
                St["TRS"].inc(s.activation(xtm[:, cl, (8 + f) * P:(9 + f) * P],
                                           pt[1][:, f * P:(f + 1) * P], AF.Copy))
            s.wait_ge(St["TRP"].h, trp_b + 17)
            St["TRS"].inc(s.activation(Btm[:, c, :], pt[0][:, 0:DS], AF.Copy))
            s.wait_ge(St["TRP"].h, trp_b + 20)
            St["TRS"].inc(s.activation(ttm[:, cl, 0, :], pa[5][:, 0:NH], AF.Copy))
            St["TRS"].inc(s.activation(ttm[:, cl, 1, :], pa[5][:, NH:2 * NH], AF.Copy,
                                       scale=-1.0))
        blk.scalar(s_trev)
        n_trs = St["TRS"].n

        def v_xd(v):
            v.wait_ge(St["TRS"].h, n_trs)
            v.wait_ge(St["TRP"].h, trp_b + 20)
            if ydq_b >= 8:
                v.wait_ge(St["YDQ"].h, ydq_b)
                v.wait_ge(St["STQ"].h, stq_b)
                v.wait_ge(St["YEV"].h, yev_b)
            St["XDP"].inc(v.tensor_tensor(ttm[:, cl, 2, :], pt[0][:, DS:DS + NH],
                                          ttm[:, cl, 1, :], ALU.mult))
            v.wait_ge(St["XDP"].h, St["XDP"].n)
            St["XDP"].inc(v.tensor_tensor(
                xdtb[:, :].rearrange("p (h e) -> p h e", h=NH),
                xtm[:, cl, :].rearrange("p (h e) -> p h e", h=NH),
                rep_free(ttm[:, cl, 1, :], HD), ALU.mult))
            St["XDP"].inc(v.tensor_tensor(
                xddb[:, :].rearrange("p (h e) -> p h e", h=NH),
                xtm[:, cl, :].rearrange("p (h e) -> p h e", h=NH),
                rep_free(ttm[:, cl, 2, :], HD), ALU.mult))
            v.wait_ge(St["IN"].h, n_dbc)
            St["XDP"].inc(v.tensor_tensor(xDq[:, :], xtm[:, cl, :], Dbc_s[:, :], ALU.mult))
        blk.vector(v_xd)

        if KCS < 2:
            return
        def t_g(t):
            t.wait_ge(St["CVS"].h, n_cvs)
            if exp_b >= 8:
                t.wait_ge(St["EXP"].h, exp_b)
            St["GTP"].inc(t.matmul(pa[0][:, 0:CHK], bccv[0:64, 0, cl * CHK:(cl + 1) * CHK],
                                   bccv[0:64, 1, cl * CHK:(cl + 1) * CHK],
                                   start=True, stop=True))
        blk.tensor(t_g)
        n_gtp = St["GTP"].n

        def v_gm(v):
            v.wait_ge(St["GTP"].h, n_gtp)
            St["GM"].inc(v.tensor_tensor(gmb[:, :], pa[0][:, 0:CHK], uinc[:, :], ALU.mult))
        blk.vector(v_gm)

        if KCS < 3:
            return
        def t_main(t):
            def segsum(q):
                if q >= 8:
                    return
                t.wait_ge(St["LH"].h, lh_b + q + 1)
                if q >= 2:
                    t.wait_ge(St["EXP"].h, exp_b + q - 1)
                if q % 2 == 0:
                    t.wait_ge(St["GM"].h, gm_b + 1)
                for j in range(4):
                    i = t.matmul(pa[q % 2][:, j * CHK:(j + 1) * CHK], lh4[:, q % 2, j, :],
                                 uinc[:, :], start=(j == 0), stop=(j == 3),
                                 skip_group_check=True)
                St["SGP"].inc(i)
            segsum(0)
            segsum(1)
            for hg in range(4):
                segsum(2 * hg + 2)
                segsum(2 * hg + 3)
                if KCS < 4:
                    continue
                for q in (2 * hg, 2 * hg + 1):
                    t.wait_ge(St["M4"].h, m4_b + q + 1)
                    t.wait_ge(St["XDP"].h, xdp_b + 2)
                    if KCS >= 7 and yev_b + hg - 1 > 0:
                        t.wait_ge(St["YEV"].h, yev_b + hg - 1)
                    for j in range(4):
                        h = 4 * q + j
                        i = t.matmul(pa[2 + hg % 2][:, (h % 8) * HD:(h % 8 + 1) * HD],
                                     m4b[:, q % 2, j, :], xdtb[:, h * HD:(h + 1) * HD],
                                     start=(q == 2 * hg and j == 0), stop=False,
                                     skip_group_check=True)
                    St["YDQ"].inc(i)
                if KCS < 5:
                    continue
                def sdoB_round(ri):
                    for jj in range(4):
                        h = 4 * ri + jj
                        t.wait_ge(St["TRS"].h, n_trs)
                        t.wait_ge(St["DTE"].h, n_dt3)
                        if jj == 0 and ch4_b + 4 * ri > 0:
                            t.wait_ge(St["CH4"].h, ch4_b + 4 * ri)
                        eh = ehc_s[:, h // 2, (h % 2) * 64:(h % 2) * 64 + 64]
                        St["SDP"].inc(t.matmul(pa[5][0:64, jj * CHK:(jj + 1) * CHK],
                                               eh, sdob[:, csl],
                                               start=(jj == 0), stop=(jj == 3),
                                               skip_group_check=True))
                sdoB_round(2 * hg)
                if KCS < 6:
                    continue
                t.wait_ge(St["XDP"].h, xdp_b + 3)
                t.wait_ge(St["ERP"].h, n_erp)
                t.wait_ge(St["SUP"].h, sup_b + 4 * hg)
                for j in range(8):
                    h = 8 * hg + j
                    i = t.matmul(pa[4][0:64, j * HD:(j + 1) * HD], Btm[:, c, :],
                                 xddb[:, h * HD:(h + 1) * HD], start=(j == 0), stop=(j == 7),
                                 skip_group_check=True)
                St["STQ"].inc(i)
                if KCS < 7:
                    continue
                def yoff_round(ri, is_last):
                    for jj in range(4):
                        h = 4 * ri + jj
                        t.wait_ge(St["CH4"].h, ch4_b + 4 * ri + jj + 1)
                        if first_chunk:
                            t.wait_ge(St["SIN"].h, 2)
                        else:
                            t.wait_ge(St["SUP"].h, sup_b - 16 + 4 * (hg + 1))
                        i = t.matmul(pa[2 + hg % 2][:, (h % 8) * HD:(h % 8 + 1) * HD],
                                     ch2b[:, jj, :],
                                     Sbf[0:64, h * HD:(h + 1) * HD],
                                     start=False, stop=(is_last and jj == 3),
                                     skip_group_check=True)
                        St["YOQ"].inc(i)
                yoff_round(2 * hg, False)
                sdoB_round(2 * hg + 1)
                yoff_round(2 * hg + 1, True)
        blk.tensor(t_main)

        if KCS < 3:
            pass
        def s_exp(s):
            s.wait_ge(St["TRS"].h, n_trs)
            for q in range(8):
                s.wait_ge(St["SGP"].h, sgp_b + q + 1)
                if q >= 1:
                    s.wait_ge(St["M4"].h, m4_b + q)
                St["EXP"].inc(s.activation(e4b[:, 0, :], pa[q % 2][:, 0:512], AF.Exp))
        blk.scalar(s_exp)

        def v_main(v):
            def lh(q):
                if q >= 8:
                    return
                v.wait_ge(St["TRS"].h, n_trs)
                if q >= 2:
                    v.wait_ge(St["SGP"].h, sgp_b + q - 1)
                for j in range(4):
                    h = 4 * q + j
                    i = v.tensor_scalar(lh4[:, q % 2, j, :], ustr[:, :],
                                        ttm[:, cl, 0, h:h + 1], None, ALU.mult)
                St["LH"].inc(i)
            lh(0)
            lh(1)
            for hg in range(4):
                lh(2 * hg + 2)
                lh(2 * hg + 3)
                for q in (2 * hg, 2 * hg + 1):
                    v.wait_ge(St["EXP"].h, exp_b + q + 1)
                    v.wait_ge(St["GM"].h, gm_b + 1)
                    if KCS >= 4 and ydq_b + q - 1 > 0:
                        v.wait_ge(St["YDQ"].h, ydq_b + q - 1)
                    St["M4"].inc(v.tensor_tensor(
                        m4b[:, q % 2, :, :],
                        e4b[:, 0, :].rearrange("p (j l) -> p j l", j=4),
                        gmb[:, :].unsqueeze(1).broadcast_to([P, 4, CHK]), ALU.mult))
                if KCS < 5:
                    continue
                for r in range(2):
                    ri = 2 * hg + r
                    for jj in range(4):
                        v.wait_ge(St["SDP"].h, sdp_b + 4 * (ri + 1))
                        if KCS >= 7 and yoq_b + 4 * ri > 0:
                            v.wait_ge(St["YOQ"].h, yoq_b + 4 * ri)
                        St["CH4"].inc(v.tensor_tensor(
                            ch2b[:, jj, :], pa[5][0:64, jj * CHK:(jj + 1) * CHK],
                            Call2[0:64, c, :], ALU.mult))
                if KCS < 6:
                    continue
                v.wait_ge(St["STQ"].h, stq_b + hg + 1)
                if KCS >= 7:
                    v.wait_ge(St["YOQ"].h, yoq_b + 8 * (hg + 1))
                ssl = slice(hg * 512, (hg + 1) * 512)
                St["SUP"].inc(v.tensor_tensor(
                    stmp[:, :].rearrange("p (h e) -> p h e", h=8),
                    Sst[:, ssl].rearrange("p (h e) -> p h e", h=8),
                    erep[:, 8 * hg:8 * hg + 8, cl].unsqueeze(2).broadcast_to([64, 8, HD]),
                    ALU.mult))
                v.wait_ge(St["SUP"].h, St["SUP"].n)
                St["SUP"].inc(v.tensor_tensor(Sst[:, ssl], stmp[:, :], pa[4][0:64, 0:512],
                                              ALU.add))
                v.wait_ge(St["SUP"].h, St["SUP"].n)
                St["SUP"].inc(v.tensor_copy(Sbf[0:64, ssl], Sst[:, ssl]))
                St["SUP"].inc(v.tensor_copy(Sbf[64:128, ssl], Sst[:, ssl]))
                if KCS < 7:
                    continue
                v.wait_ge(St["YDQ"].h, ydq_b + 2 * (hg + 1))
                v.wait_ge(St["XDP"].h, xdp_b + 4)
                if ysp_b > 0:
                    v.wait_ge(St["YSP"].h, ysp_b)
                St["YEV"].inc(v.tensor_tensor(ytm[:, ssl], pa[2 + hg % 2][:, 0:512],
                                              xDq[:, ssl], ALU.add))
        blk.vector(v_main)

        if KCS < 7:
            return
        def sy_y(sy):
            sy.wait_ge(St["YEV"].h, yev_b + 4)
            St["YSP"].inc(sy.dma_start(yspill[:, c, :], ytm[:, :]), 16)
        blk.sync(sy_y)

    def emit_exchange_and_post(acs_r, sdob, Sst, Call2, yspill):
        pe = contextlib.ExitStack()
        psb = lambda name, shape, dt=F32: pe.enter_context(nc.sbuf_tensor(name, list(shape), dt))
        sfb = psb("sfb", [64, NH * HD], BF16)
        sgat = psb("sgat", [64, NH * HD], BF16)
        sinb = psb("sinb", [64, NH * HD], BF16)
        sins = psb("sins", [P, NH * HD], BF16)
        arow = psb("arow", [32, NCHK])
        crow = psb("crow", [32, NCHK])
        lrow = psb("lrow", [32, NCHK], BF16)
        lrowf = psb("lrowf", [1, NH * NCHK], BF16)
        lamr = psb("lamr", [64, NH, NCHK])
        woall = psb("woall", [P, ICH, D], BF16)
        ztm = psb("ztm", [P, 2, DI], BF16)
        ycb = psb("ycb", [P, 2, DI], BF16)
        chc = psb("chc", [64, 4, CHK], BF16)
        ygb = psb("ygb", [P, DI], BF16)
        sqv = psb("sqv", [P, DI], BF16)
        rmsr = psb("rmsr", [P, 4])
        yfb = psb("yfb", [P, DI], BF16)
        yfm = psb("yfm", [P, 2, ICH, CHK], BF16)

        def v_sf(v):
            St["SFIN"].inc(v.tensor_copy(sfb[:, :], Sst[:, :]))
        blk.vector(v_sf)
        n_sf = St["SFIN"].n

        def sy_sf(sy):
            sy.wait_ge(St["SFIN"].h, n_sf)
            St["SFIN"].inc(sy.dma_start(ccB_in, sfb[:, :]), 16)
        blk.sync(sy_sf)
        n_sf2 = St["SFIN"].n

        def g_ccB(g):
            g.wait_ge(St["SFIN"].h, n_sf2)
            St["CCB"].inc(g.collective_compute(
                "AllGather", ALU.bypass, replica_groups=[list(range(8))],
                ins=[ccB_in], outs=[ccB_out]))
        blk.gpsimd(g_ccB)
        n_ccb = St["CCB"].n

        def v_lam(v):
            St["LAM"].inc(v.tensor_copy(arow[:, :], acs_r[:, CHK - 1::CHK]))
            v.wait_ge(St["LAM"].h, St["LAM"].n)
            St["LAM"].inc(v.tensor_tensor_scan(crow[:, :], arow[:, :], arow[:, :],
                                               0.0, ALU.add, ALU.bypass))
            v.wait_ge(St["LAM"].h, St["LAM"].n)
            St["LAM"].inc(v.tensor_tensor(crow[:, :], crow[:, :], arow[:, :], ALU.subtract))
        blk.vector(v_lam)
        n_lam1 = St["LAM"].n

        def s_lam(s):
            s.wait_ge(St["LAM"].h, n_lam1)
            St["LAM"].inc(s.activation(lrow[:, :], crow[:, :], AF.Exp))
        blk.scalar(s_lam)
        n_lam2 = St["LAM"].n

        def sy_lam(sy):
            sy.wait_ge(St["LAM"].h, n_lam2)
            St["LAM"].inc(sy.dma_start(
                lrowf[:, :].rearrange("o (h c) -> o h c", h=NH), lrow[:, :],
                allow_slow_non_contiguous=True), 16)
        blk.sync(sy_lam)
        n_lam3 = St["LAM"].n

        def t_lam(t):
            t.wait_ge(St["LAM"].h, n_lam3)
            St["LAM"].inc(t.matmul(pa[4][0:64, 0:NH * NCHK], onesb[:, 0:64], lrowf[:, :],
                                   start=True, stop=True))
        blk.tensor(t_lam)
        n_lam4 = St["LAM"].n

        def v_lam2(v):
            v.wait_ge(St["LAM"].h, n_lam4)
            St["LAM"].inc(v.tensor_copy(
                lamr[:, :, :], pa[4][0:64, 0:NH * NCHK].rearrange("p (h c) -> p h c", h=NH)))
        blk.vector(v_lam2)
        n_lam = St["LAM"].n

        def g_wo(g):
            for k in range(ICH):
                St["WOD"].inc(g.dma_start(woall[:, k, :], WoutT[k * P:(k + 1) * P, :]), 16)
        blk.gpsimd(g_wo)
        n_wod = St["WOD"].n

        def v_wo(v):
            v.wait_ge(St["WOD"].h, n_wod)
            for k in range(ICH):
                St["WOP"].inc(v.tensor_scalar(woall[:, k, :], woall[:, k, :],
                                              nwp_s[:, k:k + 1], None, ALU.mult))
        blk.vector(v_wo)
        n_wop = St["WOP"].n

        sga_b = St["SGA"].n
        sinb_b = St["SINB"].n

        def sy_sg2(sy):
            sy.wait_ge(St["CCB"].h, n_ccb)
            for j in range(8):
                if j >= 1:
                    sy.wait_ge(St["SINB"].h, sinb_b + j)
                St["SGA"].inc(sy.dma_start(sgat[:, :], ccB_out[j]), 16)
        blk.sync(sy_sg2)

        def v_sin(v):
            for j in range(8):
                v.wait_ge(St["SGA"].h, sga_b + 16 * (j + 1))
                if j == 0:
                    St["SINB"].inc(v.tensor_scalar(sinb[:, :], sgat[:, :],
                                                   sel_s[0:64, 0:1], None, ALU.mult))
                else:
                    St["SINB"].inc(v.scalar_tensor_tensor(
                        sinb[:, :], sgat[:, :], sel_s[0:64, j:j + 1], sinb[:, :],
                        ALU.mult, ALU.add))
                v.wait_ge(St["SINB"].h, St["SINB"].n)
        blk.vector(v_sin)
        n_sinb = St["SINB"].n

        for c in range(NCHK):
            csl = slice(c * CHK, (c + 1) * CHK)
            zrs, zrcs = ("ZR", "ZRC") if c % 2 == 0 else ("ZRB", "ZRD")
            zr_b, zrc_b = St[zrs].n, St[zrcs].n
            corp_b, corev_b = St["CORP"].n, St["COREV"].n
            ch4_b, sdp_b = St["CH4"].n, St["SDP"].n
            ppq_b, rms_b = St["PPQ"].n, St["RMS"].n
            yfp_b, yfs_b = St["YFP"].n, St["YFS"].n
            wop_b2, xr_b = St["WOP"].n, St["XR"].n
            scs_b = St["SCS"].n

            def sy_yz(sy, _c=c):
                if _c == 0:
                    sy.wait_ge(St["LAM"].h, n_lam3)
                if ppq_b - 4 > 0:
                    sy.wait_ge(St["PPQ"].h, ppq_b - 4)
                St[zrs].inc(sy.dma_start(ycb[:, c % 2, :], yspill[:, c, :]), 16)
                St[zrcs].inc(sy.dma_start(ztm[:, c % 2, :],
                                          zspill[:, c * CHK:(c + 1) * CHK], transpose=True), 16)
            blk.sync(sy_yz)

            def v_sins(v, _c=c):
                v.wait_ge(St["SINB"].h, n_sinb)
                v.wait_ge(St["LAM"].h, n_lam)
                if corp_b > 0:
                    v.wait_ge(St["CORP"].h, corp_b)
                St["SCS"].inc(v.tensor_tensor(
                    sins[0:64, :].rearrange("p (h e) -> p h e", h=NH),
                    sinb[:, :].rearrange("p (h e) -> p h e", h=NH),
                    lamr[:, :, _c].unsqueeze(2).broadcast_to([64, NH, HD]), ALU.mult))
                v.wait_ge(St["SCS"].h, St["SCS"].n)
                St["SCS"].inc(v.tensor_copy(sins[64:128, :], sins[0:64, :]))
            blk.vector(v_sins)
            n_scs2 = St["SCS"].n

            def t_cor2(t, _c=c):
                t.wait_ge(St["SCS"].h, n_scs2)
                for ri in range(8):
                    hg = ri // 2
                    for jj in range(4):
                        h = 4 * ri + jj
                        if jj == 0 and ch4_b + 4 * ri > 0:
                            t.wait_ge(St["CH4"].h, ch4_b + 4 * ri)
                        eh = ehc_s[:, h // 2, (h % 2) * 64:(h % 2) * 64 + 64]
                        St["SDP"].inc(t.matmul(pa[2][0:64, jj * CHK:(jj + 1) * CHK],
                                               eh, sdob[:, csl],
                                               start=(jj == 0), stop=(jj == 3),
                                               skip_group_check=True))
                    for jj in range(4):
                        h = 4 * ri + jj
                        t.wait_ge(St["CH4"].h, ch4_b + 4 * ri + jj + 1)
                        if corev_b + hg > 0:
                            t.wait_ge(St["COREV"].h, corev_b + hg)
                        i = t.matmul(pa[3][:, (h % 8) * HD:(h % 8 + 1) * HD],
                                     chc[:, jj, :],
                                     sins[0:64, h * HD:(h + 1) * HD],
                                     start=(ri % 2 == 0 and jj == 0),
                                     stop=(ri % 2 == 1 and jj == 3),
                                     skip_group_check=True)
                        St["CORP"].inc(i)
            blk.tensor(t_cor2)

            def v_ch(v, _c=c, _zrs=zrs):
                for hg in range(4):
                    for r in range(2):
                        ri = 2 * hg + r
                        for jj in range(4):
                            v.wait_ge(St["SDP"].h, sdp_b + 4 * (ri + 1))
                            if corp_b + 4 * ri > 0:
                                v.wait_ge(St["CORP"].h, corp_b + 4 * ri)
                            St["CH4"].inc(v.tensor_tensor(
                                chc[:, jj, :], pa[2][0:64, jj * CHK:(jj + 1) * CHK],
                                Call2[0:64, _c, :], ALU.mult))
                    v.wait_ge(St[_zrs].h, zr_b + 16)
                    v.wait_ge(St["CORP"].h, corp_b + 8 * (hg + 1))
                    ssl = slice(hg * 512, (hg + 1) * 512)
                    St["COREV"].inc(v.tensor_tensor(ycb[:, _c % 2, ssl], ycb[:, _c % 2, ssl],
                                                    pa[3][:, 0:512], ALU.add))
            blk.vector(v_ch)
            n_corev = St["COREV"].n

            def s_pp(s, _c=c, _zrcs=zrcs):
                s.wait_ge(St[_zrcs].h, zrc_b + 16)
                St["PPQ"].inc(s.activation(ztm[:, _c % 2, :], ztm[:, _c % 2, :], AF.Silu))
            blk.scalar(s_pp)
            n_pp1 = St["PPQ"].n

            def v_pp(v, _c=c):
                v.wait_ge(St["PPQ"].h, n_pp1)
                v.wait_ge(St["COREV"].h, n_corev)
                if rms_b > 0:
                    v.wait_ge(St["RMS"].h, rms_b)
                St["PPQ"].inc(v.tensor_tensor(ygb[:, :], ycb[:, _c % 2, :],
                                              ztm[:, _c % 2, :], ALU.mult))
            blk.vector(v_pp)
            n_pp2 = St["PPQ"].n

            def s_sq(s):
                s.wait_ge(St["PPQ"].h, n_pp2)
                St["PPQ"].inc(s.activation(sqv[:, :], ygb[:, :], AF.Square))
            blk.scalar(s_sq)
            n_pp3 = St["PPQ"].n

            def v_rms(v):
                v.wait_ge(St["PPQ"].h, n_pp3)
                St["RMS"].inc(v.tensor_reduce(rmsr[:, 0:1], sqv[:, :], mybir.AxisListType.X,
                                              ALU.add))
                v.wait_ge(St["RMS"].h, St["RMS"].n)
                St["RMS"].inc(v.tensor_scalar(rmsr[:, 1:2], rmsr[:, 0:1], 1.0 / DI, None,
                                              ALU.mult, ))
                v.wait_ge(St["RMS"].h, St["RMS"].n)
                St["RMS"].inc(v.tensor_scalar(rmsr[:, 1:2], rmsr[:, 1:2], EPS, None, ALU.add))
                v.wait_ge(St["RMS"].h, St["RMS"].n)
                St["RMS"].inc(v.reciprocal(rmsr[:, 2:3], rmsr[:, 1:2]))
            blk.vector(v_rms)
            n_rms1 = St["RMS"].n

            def s_rms(s):
                s.wait_ge(St["RMS"].h, n_rms1)
                St["RMS"].inc(s.activation(rmsr[:, 3:4], rmsr[:, 2:3], AF.Sqrt))
            blk.scalar(s_rms)
            n_rms2 = St["RMS"].n

            def v_yf(v):
                v.wait_ge(St["RMS"].h, n_rms2)
                if yfp_b > 0:
                    v.wait_ge(St["YFP"].h, yfp_b)
                St["RMS"].inc(v.tensor_scalar(yfb[:, :], ygb[:, :], rmsr[:, 3:4], None,
                                              ALU.mult))
            blk.vector(v_yf)
            n_rms = St["RMS"].n

            def t_yf(t, _c=c):
                t.wait_ge(St["RMS"].h, n_rms)
                if yfs_b >= 16:
                    t.wait_ge(St["YFS"].h, yfs_b)
                for k in range(8):
                    St["YFP"].inc(t.transpose(pt[0][:, k * P:(k + 1) * P],
                                              yfb[:, k * P:(k + 1) * P], identb[:, :]))
                for k in range(8):
                    St["YFP"].inc(t.transpose(pt[1][:, k * P:(k + 1) * P],
                                              yfb[:, (8 + k) * P:(9 + k) * P], identb[:, :]))
            blk.tensor(t_yf)

            def s_yfs(s, _c=c):
                s.wait_ge(St["YFP"].h, yfp_b + 8)
                for k in range(8):
                    St["YFS"].inc(s.activation(yfm[:, _c % 2, k, :],
                                               pt[0][:, k * P:(k + 1) * P], AF.Copy))
                s.wait_ge(St["YFP"].h, yfp_b + 16)
                for k in range(8):
                    St["YFS"].inc(s.activation(yfm[:, _c % 2, 8 + k, :],
                                               pt[1][:, k * P:(k + 1) * P], AF.Copy))
            blk.scalar(s_yfs)
            n_yfs = St["YFS"].n

            def t_wo(t, _c=c):
                t.wait_ge(St["YFS"].h, n_yfs)
                t.wait_ge(St["WOP"].h, n_wop)
                for d in range(FCH):
                    if xr_b + d - 1 > 0:
                        t.wait_ge(St["XR"].h, xr_b + d - 1)
                    for k in range(ICH):
                        i = t.matmul(pa[d % 2][:, 0:CHK], woall[:, k, d * P:(d + 1) * P],
                                     yfm[:, _c % 2, k, :], start=(k == 0), stop=(k == ICH - 1))
                    St["WOP"].inc(i)
            blk.tensor(t_wo)
            n_wop2 = St["WOP"].n

            def v_xup2(v, _c=c, _w=n_wop2 - FCH):
                for d in range(FCH):
                    v.wait_ge(St["WOP"].h, _w + d + 1)
                    St["XR"].inc(v.tensor_tensor(
                        xbuf[:, d, _c * CHK:(_c + 1) * CHK], xbuf[:, d, _c * CHK:(_c + 1) * CHK],
                        pa[d % 2][:, 0:CHK], ALU.add))
            blk.vector(v_xup2)
        pe.close()

    def mamba_sublayer():
        arena = contextlib.ExitStack()
        asb = lambda name, shape, dt=F32: arena.enter_context(nc.sbuf_tensor(name, list(shape), dt))
        acs_r = asb("acs_r", [32, T])
        sdob = asb("sdob", [32, T], BF16)
        Sst = asb("Sst", [64, NH * HD])
        Sbf = asb("Sbf", [P, NH * HD], BF16)
        Call2 = asb("Call2", [P, NCHK, CHK], BF16)
        gparena = contextlib.ExitStack()
        gpsb = lambda name, shape, dt=F32: gparena.enter_context(nc.sbuf_tensor(name, list(shape), dt))
        u_r = gpsb("u_r", [32, T])
        adt_r = gpsb("adt_r", [32, T])
        ytm = gpsb("ytm", [P, DI], BF16)
        Btm = gpsb("Btm", [P, NCHK, DS], BF16)
        carry = gpsb("carry", [P, 18, 3], BF16)
        wib = gpsb("wib", [P, 2, FCH, P], BF16)
        xng = gpsb("xng", [P, FCH, GT], BF16)
        zstg = gpsb("zstg", [P, 2, GT], BF16)
        Dbc_s = gpsb("Dbc_s", [P, DI], BF16)

        def sy_dbc(sy):
            St["IN"].inc(sy.dma_start(Dbc_s[:, :], Dbc), 16)
        blk.sync(sy_dbc)
        n_dbc = St["IN"].n

        WIN_XBC0 = DI

        harena = contextlib.ExitStack()
        hga = harena.enter_context(nc.sbuf_tensor("hga", [P, 8, 64], F32))
        halos = harena.enter_context(nc.sbuf_tensor("halos", [P, 64], F32))
        xn3 = harena.enter_context(nc.sbuf_tensor("xn3", [P, FCH, 4], BF16))
        hsrc = harena.enter_context(nc.sbuf_tensor("hsrc", [P, 64], F32))
        ln_tile(T - 3, lambda f: xn3[:, f, 0:3], w=3)
        n_lnx3 = St["LNX"].n
        wiA_b, wiB_b = St["W1A"].n, St["W1B"].n

        zev_b = St["ZEV"].n

        def g_wtiny2(g):
            for o in range(17):
                if o >= 2:
                    g.wait_ge(St["ZEV"].h, zev_b + (o - 1))
                St["W1A" if o % 2 == 0 else "W1B"].inc(g.dma_start(
                    wib[:, o % 2],
                    WinT[:, WIN_XBC0 + o * P: WIN_XBC0 + (o + 1) * P]
                    .rearrange("(kc p) j -> p kc j", p=P)), 16)
        blk.gpsimd(g_wtiny2)

        hp_b = St["HP"].n
        def t_wtiny(t):
            t.wait_ge(St["LNX"].h, n_lnx3)
            for o in range(17):
                par = "W1A" if o % 2 == 0 else "W1B"
                t.wait_ge(St[par].h, (wiA_b if o % 2 == 0 else wiB_b) + 16 * (o // 2 + 1))
                if o >= 2:
                    t.wait_ge(St["ZEV"].h, zev_b + (o - 1))
                if o < 16:
                    for f in range(FCH):
                        i = t.matmul(pa[o % 2][:, 0:3], wib[:, o % 2, f, :],
                                     xn3[:, f, 0:3], start=(f == 0), stop=(f == FCH - 1))
                    St["HP"].inc(i)
                else:
                    t.wait_ge(St["ZEV"].h, zev_b + 16)
                    for half in range(2):
                        for f in range(FCH):
                            i = t.matmul(pa[half][0:64, 0:3],
                                         wib[:, o % 2, f, half * 64:(half + 1) * 64],
                                         xn3[:, f, 0:3], start=(f == 0), stop=(f == FCH - 1))
                        St["HP"].inc(i)
        blk.tensor(t_wtiny)

        def v_hz(v):
            St["HSRC"].inc(v.memset(hsrc[:, :], 0.0))
        blk.vector(v_hz)
        n_hz = St["HSRC"].n

        def s_htiny(s):
            s.wait_ge(St["HSRC"].h, n_hz)
            for o in range(16):
                s.wait_ge(St["HP"].h, hp_b + o + 1)
                St["ZEV"].inc(s.activation(hsrc[:, o * 3:(o + 1) * 3], pa[o % 2][:, 0:3], AF.Copy))
            s.wait_ge(St["HP"].h, hp_b + 18)
            St["ZEV"].inc(s.activation(hsrc[0:64, 48:51], pa[0][0:64, 0:3], AF.Copy))
            St["ZEV"].inc(s.activation(hsrc[0:64, 51:54], pa[1][0:64, 0:3], AF.Copy))
        blk.scalar(s_htiny)
        n_hs = St["ZEV"].n

        def sy_ccA(sy):
            sy.wait_ge(St["ZEV"].h, n_hs)
            St["HSRC"].inc(sy.dma_start(ccA_in, hsrc[:, :]), 16)
        blk.sync(sy_ccA)
        n_hsrc = St["HSRC"].n

        def g_ccA(g):
            g.wait_ge(St["HSRC"].h, n_hsrc)
            St["CCA"].inc(g.collective_compute(
                "AllGather", ALU.bypass, replica_groups=[list(range(8))],
                ins=[ccA_in], outs=[ccA_out]))
        blk.gpsimd(g_ccA)
        n_cca = St["CCA"].n

        def sy_hga(sy):
            sy.wait_ge(St["CCA"].h, n_cca)
            St["HGA"].inc(sy.dma_start(hga[:, :, :], ccA_out.rearrange("r p c -> p r c")), 16)
        blk.sync(sy_hga)
        n_hga = St["HGA"].n

        def v_hblend(v):
            v.wait_ge(St["HGA"].h, n_hga)
            St["HLO"].inc(v.tensor_scalar(halos[:, :], hga[:, 0, :], sel_s[:, 0:1], None, ALU.mult))
            for j in range(1, 8):
                v.wait_ge(St["HLO"].h, St["HLO"].n)
                St["HLO"].inc(v.scalar_tensor_tensor(
                    halos[:, :], hga[:, j, :], sel_s[:, j:j + 1], halos[:, :],
                    ALU.mult, ALU.add))
            v.wait_ge(St["HLO"].h, St["HLO"].n)
            St["HLO"].inc(v.tensor_copy(
                carry[:, 0:16, :], halos[:, 0:48].rearrange("p (a b) -> p a b", a=16)))
            St["HLO"].inc(v.tensor_copy(carry[0:64, 16, :], halos[0:64, 48:51]))
            St["HLO"].inc(v.tensor_copy(carry[0:64, 17, :], halos[0:64, 51:54]))
        blk.vector(v_hblend)
        n_hlo = St["HLO"].n

        def v_s0(v):
            St["SIN"].inc(v.memset(Sst[:, :], 0.0))
            St["SIN"].inc(v.memset(Sbf[:, :], 0.0))
        blk.vector(v_s0)
        harena.close()

        OLIST = []
        for o in range(16):
            OLIST.append((o * P, P, "z", o))
        for o in range(16):
            OLIST.append((DI + o * P, P, "x", o))
        OLIST.append((DI + 2048, 64, "B", 0))
        OLIST.append((DI + 2048 + 64, 64, "C", 0))
        OLIST.append((4224, 32, "dt", 0))

        for g in range(NG if m_stage >= 2 else 0):
            t0 = g * GT
            garena = contextlib.ExitStack()
            gsb = lambda name, shape, dt=F32: garena.enter_context(
                nc.sbuf_tensor(name + str(g), list(shape), dt))
            xcv = gsb("xcv", [P, 16, GT], BF16)
            bccv = gsb("bccv", [64, 2, GT], BF16)
            erep = gsb("erep", [64, 32, CPG], F32)

            ln_tile(t0, lambda f: xng[:, f, 0:GT], w=GT)
            n_lnxg = St["LNX"].n

            warena = contextlib.ExitStack()
            xbcr = warena.enter_context(nc.sbuf_tensor(f"xbcr{g}", [P, 16, GT + 3], BF16))
            bcraw = warena.enter_context(nc.sbuf_tensor(f"bcraw{g}", [64, 2, GT + 3], BF16))
            erow = warena.enter_context(nc.sbuf_tensor(f"erow{g}", [32, CPG], BF16))
            erowf = warena.enter_context(nc.sbuf_tensor(f"erowf{g}", [1, 32 * CPG], BF16))

            wiA_b2, wiB_b2 = St["W1A"].n, St["W1B"].n
            zev_b2 = St["ZEV"].n
            nol = len(OLIST)

            def g_win(g_, _z=zev_b2):
                for oi, (c0, mrows, kind, idx) in enumerate(OLIST):
                    if oi >= 2:
                        g_.wait_ge(St["ZEV"].h, _z + (oi - 1))
                    St["W1A" if oi % 2 == 0 else "W1B"].inc(g_.dma_start(
                        wib[:, oi % 2, :, 0:mrows],
                        WinT[:, c0:c0 + mrows].rearrange("(kc p) j -> p kc j", p=P)), 16)
            blk.gpsimd(g_win)

            hp_b2 = St["HP"].n
            def t_win(t, _z=zev_b2, _hp=hp_b2):
                t.wait_ge(St["LNX"].h, n_lnxg)
                for oi, (c0, mrows, kind, idx) in enumerate(OLIST):
                    par = "W1A" if oi % 2 == 0 else "W1B"
                    t.wait_ge(St[par].h, (wiA_b2 if oi % 2 == 0 else wiB_b2) + 16 * (oi // 2 + 1))
                    if oi >= 2:
                        t.wait_ge(St["ZEV"].h, _z + (oi - 1))
                    for f in range(FCH):
                        i = t.matmul(pa[oi % 2][0:mrows, 0:GT],
                                     wib[:, oi % 2, f, 0:mrows],
                                     xng[:, f, 0:GT], start=(f == 0), stop=(f == FCH - 1))
                    St["HP"].inc(i)
            blk.tensor(t_win)

            zs_b = [St["ZSA"].n, St["ZSB"].n]
            if m_stage < 2.5:
                blk.tensor(t_win) if False else None
            def s_wev(s, _hp=hp_b2, _zs=zs_b):
                for oi, (c0, mrows, kind, idx) in enumerate(OLIST):
                    s.wait_ge(St["HP"].h, _hp + oi + 1)
                    bank = pa[oi % 2]
                    if kind == "z":
                        if idx >= 2:
                            s.wait_ge(St["ZSA" if idx % 2 == 0 else "ZSB"].h,
                                      _zs[idx % 2] + 16 * (idx // 2))
                        St["ZEV"].inc(s.activation(zstg[:, idx % 2, :], bank[:, 0:GT], AF.Copy))
                    elif kind == "x":
                        St["ZEV"].inc(s.activation(xbcr[:, idx, 3:3 + GT], bank[:, 0:GT], AF.Copy))
                    elif kind in ("B", "C"):
                        St["ZEV"].inc(s.activation(
                            bcraw[0:64, 0 if kind == "B" else 1, 3:3 + GT],
                            bank[0:64, 0:GT], AF.Copy))
                    else:
                        St["ZEV"].inc(s.activation(u_r[:, t0:t0 + GT], bank[0:32, 0:GT], AF.Copy))
            blk.scalar(s_wev)
            n_zev_g = St["ZEV"].n

            def sy_zs(sy, _zev=zev_b2, _t0=t0):
                for idx in range(16):
                    sy.wait_ge(St["ZEV"].h, _zev + idx + 1)
                    St["ZSA" if idx % 2 == 0 else "ZSB"].inc(sy.dma_start(
                        zspill[idx * P:(idx + 1) * P, _t0:_t0 + GT], zstg[:, idx % 2, :]), 16)
            blk.sync(sy_zs)

            if m_stage < 3:
                barrier()
                warena.close()
                garena.close()
                continue
            def s_dt(s, _t0=t0):
                s.wait_ge(St["ZEV"].h, n_zev_g)
                St["DTE"].inc(s.activation(u_r[:, _t0:_t0 + GT], u_r[:, _t0:_t0 + GT],
                                           AF.Sigmoid, scale=-1.0, bias=ndtb_s[:, :]))
                s.wait_ge(St["DTE"].h, St["DTE"].n)
                St["DTE"].inc(s.activation(u_r[:, _t0:_t0 + GT], u_r[:, _t0:_t0 + GT], AF.Ln))
            blk.scalar(s_dt)
            n_dt1 = St["DTE"].n

            def v_adt(v, _t0=t0):
                v.wait_ge(St["DTE"].h, n_dt1)
                St["DTE"].inc(v.tensor_scalar(adt_r[:, _t0:_t0 + GT], u_r[:, _t0:_t0 + GT],
                                              nA_s[:, :], None, ALU.mult))
                v.wait_ge(St["DTE"].h, St["DTE"].n)
                for cl in range(CPG):
                    csl = slice(_t0 + cl * CHK, _t0 + (cl + 1) * CHK)
                    St["DTE"].inc(v.tensor_tensor_scan(
                        acs_r[:, csl], adt_r[:, csl], adt_r[:, csl], 0.0, ALU.add, ALU.bypass))
            blk.vector(v_adt)
            n_dt2 = St["DTE"].n

            def s_sdo(s, _t0=t0):
                s.wait_ge(St["DTE"].h, n_dt2)
                St["DTE"].inc(s.activation(sdob[:, _t0:_t0 + GT], acs_r[:, _t0:_t0 + GT], AF.Exp))
            blk.scalar(s_sdo)
            n_dt3 = St["DTE"].n

            def s_er(s, _t0=t0):
                s.wait_ge(St["DTE"].h, n_dt2)
                St["ERP"].inc(s.activation(
                    erow[:, :], acs_r[:, _t0 + CHK - 1:_t0 + GT:CHK], AF.Exp))
            blk.scalar(s_er)
            n_er1 = St["ERP"].n

            def sy_er(sy):
                sy.wait_ge(St["ERP"].h, n_er1)
                St["ERP"].inc(sy.dma_start(
                    erowf[:, :].rearrange("o (h c) -> o h c", h=32),
                    erow[:, :], allow_slow_non_contiguous=True), 16)
            blk.sync(sy_er)
            n_er2 = St["ERP"].n

            def t_er(t):
                t.wait_ge(St["ERP"].h, n_er2)
                St["ERP"].inc(t.matmul(pa[4][0:64, 0:32 * CPG], onesb[:, 0:64], erowf[:, :],
                                       start=True, stop=True))
            blk.tensor(t_er)
            n_er3 = St["ERP"].n

            def v_er(v):
                v.wait_ge(St["ERP"].h, n_er3)
                St["ERP"].inc(v.tensor_copy(
                    erep[:, :, :], pa[4][0:64, 0:32 * CPG].rearrange("p (h c) -> p h c", h=32)))
            blk.vector(v_er)

            if m_stage < 4:
                warena.close()
                barrier()
                garena.close()
                continue
            def v_carry(v, _g=g):
                if _g == 0:
                    v.wait_ge(St["HLO"].h, n_hlo)
                v.wait_ge(St["CAR"].h, St["CAR"].n)
                v.wait_ge(St["ZEV"].h, n_zev_g)
                St["CV"].inc(v.tensor_copy(xbcr[:, :, 0:3], carry[:, 0:16, :]))
                St["CV"].inc(v.tensor_copy(bcraw[0:64, 0, 0:3], carry[0:64, 16, :]))
                St["CV"].inc(v.tensor_copy(bcraw[0:64, 1, 0:3], carry[0:64, 17, :]))
            blk.vector(v_carry)
            n_cv0 = St["CV"].n

            def v_conv(v):
                v.wait_ge(St["CV"].h, n_cv0)
                St["CAR"].inc(v.tensor_copy(carry[:, 0:16, :], xbcr[:, :, GT:GT + 3]))
                St["CAR"].inc(v.tensor_copy(carry[0:64, 16, :], bcraw[0:64, 0, GT:GT + 3]))
                St["CAR"].inc(v.tensor_copy(carry[0:64, 17, :], bcraw[0:64, 1, GT:GT + 3]))
                for slot in range(18):
                    if slot < 16:
                        src = lambda k, _s=slot: xbcr[:, _s, k:k + GT]
                        dst = xcv[:, slot, :]
                        wcol = lambda k: convw_s[:, slot, k:k + 1]
                    else:
                        src = lambda k, _s=slot: bcraw[0:64, _s - 16, k:k + GT]
                        dst = bccv[:, slot - 16, :]
                        wcol = lambda k, _s=slot: convw_s[0:64, _s, k:k + 1]
                    St["CV"].inc(v.tensor_scalar(dst, src(0), wcol(0), None, ALU.mult))
                    for k in range(1, 4):
                        v.wait_ge(St["CV"].h, St["CV"].n)
                        St["CV"].inc(v.scalar_tensor_tensor(
                            dst, src(k), wcol(k), dst, ALU.mult, ALU.add))
            blk.vector(v_conv)
            n_cv1 = St["CV"].n
            n_car1 = St["CAR"].n

            def s_silu(s):
                s.wait_ge(St["CV"].h, n_cv1)
                s.wait_ge(St["CAR"].h, n_car1)
                for slot in range(16):
                    St["CVS"].inc(s.activation(xcv[:, slot, :], xcv[:, slot, :], AF.Silu,
                                               bias=convb_s[:, slot:slot + 1]))
                St["CVS"].inc(s.activation(bccv[:, 0, :], bccv[:, 0, :], AF.Silu,
                                           bias=convb_s[0:64, 16:17]))
                St["CVS"].inc(s.activation(bccv[:, 1, :], bccv[:, 1, :], AF.Silu,
                                           bias=convb_s[0:64, 17:18]))
            blk.scalar(s_silu)
            n_cvs = St["CVS"].n

            def v_call(v, _g0=g):
                v.wait_ge(St["CVS"].h, n_cvs)
                c0 = _g0 * CPG
                St["SCS"].inc(v.tensor_copy(
                    Call2[0:64, c0:c0 + CPG, :],
                    bccv[:, 1, :].rearrange("p (c l) -> p c l", c=CPG)))
                St["SCS"].inc(v.tensor_copy(
                    Call2[64:128, c0:c0 + CPG, :],
                    bccv[:, 1, :].rearrange("p (c l) -> p c l", c=CPG)))
            blk.vector(v_call)

            warena.close()

            xtm = garena.enter_context(nc.sbuf_tensor(f"xtm{g}", [P, CPG, DI], BF16))
            ttm = garena.enter_context(nc.sbuf_tensor(f"ttm{g}", [P, CPG, 3, NH], F32))
            dcyf = garena.enter_context(nc.sbuf_tensor(f"dcyf{g}", [32, CHK], BF16))
            lh4 = garena.enter_context(nc.sbuf_tensor(f"lh4{g}", [P, 2, 4, CHK], BF16))
            e4b = garena.enter_context(nc.sbuf_tensor(f"e4b{g}", [P, 1, 512], BF16))
            m4b = garena.enter_context(nc.sbuf_tensor(f"m4b{g}", [P, 2, 4, CHK], BF16))
            gmb = garena.enter_context(nc.sbuf_tensor(f"gmb{g}", [P, CHK], BF16))
            ch2b = garena.enter_context(nc.sbuf_tensor(f"ch2b{g}", [64, 4, CHK], BF16))
            xdtb = garena.enter_context(nc.sbuf_tensor(f"xdtb{g}", [P, DI], BF16))
            xddb = garena.enter_context(nc.sbuf_tensor(f"xddb{g}", [P, DI], BF16))
            xDq = garena.enter_context(nc.sbuf_tensor(f"xDq{g}", [P, DI], BF16))
            stmp = garena.enter_context(nc.sbuf_tensor(f"stmp{g}", [64, 512], F32))

            n_erp = St["ERP"].n
            for cl in range(CPG if m_stage >= 5 else 0):
                c = g * CPG + cl
                csl = slice(t0 + cl * CHK, t0 + (cl + 1) * CHK)
                emit_chunk(c, cl, csl, g, t0, xcv, bccv, xtm, ttm, dcyf, lh4, e4b, m4b,
                           gmb, ch2b, xdtb, xddb, xDq, stmp, erep,
                           u_r, adt_r, acs_r, sdob, ytm, Sst, Sbf, Call2, Btm,
                           yspill, Dbc_s, n_dbc, n_cvs, n_dt3, n_erp, first_chunk=(c == 0))
            barrier()

            garena.close()

        gparena.close()
        if m_stage >= 6:
            emit_exchange_and_post(acs_r, sdob, Sst, Call2, yspill)
        arena.close()
        barrier()

    def phase_out():
        def sy_body(sy):
            St["OUT"].inc(sy.dma_start(yout, xbuf[:, :, :]), 16)
            sy.wait_ge(St["OUT"].h, St["OUT"].n)
        blk.sync(sy_body)

    phase_load()
    ffn_sublayer("f1")
    if n_sub >= 2:
        mamba_sublayer()
    if n_sub >= 3:
        ffn_sublayer("f2")
    phase_out()

    ctx.close()
    return nc



def prep_inputs(inputs):
    u = np.ascontiguousarray(inputs["L0_attn_u"])
    WinT = np.ascontiguousarray(inputs["L0_attn_W_in"].T)
    WoutT = np.ascontiguousarray(inputs["L0_attn_W_out"].T)
    conv_w = np.ascontiguousarray(inputs["L0_attn_conv_w"])
    conv_b = np.ascontiguousarray(inputs["L0_attn_conv_b"])
    dt_bias = np.ascontiguousarray(inputs["L0_attn_dt_bias"]).reshape(32, 1)
    A_log = np.ascontiguousarray(inputs["L0_attn_A_log"]).reshape(32, 1)
    Dv = np.ascontiguousarray(inputs["L0_attn_D"])
    norm_w = np.ascontiguousarray(inputs["L0_attn_norm_w"])
    gate_w = np.ascontiguousarray(inputs["L0_ffn_gate_w"])
    w1 = np.ascontiguousarray(inputs["L0_ffn_w1"])
    b1 = np.ascontiguousarray(inputs["L0_ffn_b1"])
    w2 = np.ascontiguousarray(inputs["L0_ffn_w2"])
    b2 = np.ascontiguousarray(inputs["L0_ffn_b2"])
    sw1 = np.ascontiguousarray(inputs["L0_ffn_sw1"])
    sb1 = np.ascontiguousarray(inputs["L0_ffn_sb1"])
    sw2 = np.ascontiguousarray(inputs["L0_ffn_sw2"])
    sb2 = np.ascontiguousarray(inputs["L0_ffn_sb2"])

    gwT = np.ascontiguousarray(gate_w.T.reshape(FCH, P, 2).transpose(1, 0, 2))
    b1pk = np.ascontiguousarray(b1.reshape(2, MCH, P).transpose(0, 2, 1))
    sb1pk = np.ascontiguousarray(sb1.reshape(MCH, P).T)
    b2pk = np.ascontiguousarray(b2.reshape(2, FCH, P).transpose(0, 2, 1))
    sb2pk = np.ascontiguousarray(sb2.reshape(FCH, P).T)
    cw = conv_w.reshape(2176, 4)
    convw_pk = np.zeros((P, 18, 4), np.float32)
    convw_pk[:, :17, :] = cw.reshape(17, P, 4).transpose(1, 0, 2)
    convw_pk[0:64, 16, :] = cw[2048:2112]
    convw_pk[0:64, 17, :] = cw[2112:2176]
    convb_pk = np.zeros((P, 18), np.float32)
    convb_pk[:, :17] = conv_b.reshape(17, P).T
    convb_pk[0:64, 16] = conv_b[2048:2112]
    convb_pk[0:64, 17] = conv_b[2112:2176]
    Dbc = np.repeat(Dv[None, :], P, 0)
    Dbc_full = np.ascontiguousarray(
        np.broadcast_to(np.repeat(Dv, HD)[None, :], (P, DI))).astype(BF)
    nwpk = np.ascontiguousarray(norm_w.reshape(ICH, P).T)
    ones_bf = np.ones((1, P), BF)
    onesc_bf = np.ones((P, 1), BF)
    ident_bf = np.eye(P, dtype=BF)
    ident_f32 = np.eye(P, dtype=np.float32)
    ehc2 = np.zeros((32, ICH, P), BF)
    for hh in range(ICH):
        ehc2[2 * hh, hh, 0:64] = 1
        ehc2[2 * hh + 1, hh, 64:128] = 1
    ki, li = np.meshgrid(np.arange(P), np.arange(P), indexing="ij")
    ustrict = (ki > li).astype(BF)
    uincl = (ki <= li).astype(BF)

    shared = dict(WinT=WinT, WoutT=WoutT, w1e=w1, w2e=w2, sw1=sw1, sw2=sw2,
                  gwT=gwT, b1p=b1pk, sb1p=sb1pk, b2p=b2pk, sb2p=sb2pk,
                  convw=convw_pk, convb=convb_pk, dtb=dt_bias, alog=A_log,
                  Dbc=Dbc_full, nwp=nwpk, ones_bf=ones_bf, onesc_bf=onesc_bf, ident_bf=ident_bf,
                  ident_f32=ident_f32, ustrict=ustrict, uincl=uincl, ehc2=ehc2)

    in_maps = []
    for core in range(8):
        b, half = core // 2, core % 2
        xs = u[b, half * T:(half + 1) * T, :]
        xin = np.ascontiguousarray(
            xs.reshape(T, FCH, P).transpose(2, 1, 0))
        selm = np.zeros((P, 8), np.float32)
        if half == 1:
            selm[:, core - 1] = 1.0
        m = dict(shared)
        m["xin"] = xin
        m["sel"] = selm
        in_maps.append(m)
    return in_maps


def kernel(**inputs):
    in_maps = prep_inputs(inputs)
    nc = build_nc()
    res = run_bass_kernel_spmd(nc, in_maps, core_ids=list(range(8)))
    out = np.empty((4, 4096, 1024), np.float32)
    for core in range(8):
        b, half = core // 2, core % 2
        y = res.results[core]["yout"]
        out[b, half * T:(half + 1) * T, :] = y.transpose(2, 1, 0).reshape(T, D)
    return out
